# version 01 (final)
# baseline (speedup 1.0000x reference)
import numpy as np
import concourse.bass as bass
import concourse.mybir as mybir
from concourse.bass_utils import run_bass_kernel_spmd

F32 = mybir.dt.float32
AF = mybir.ActivationFunctionType
ALU = mybir.AluOpType
AX = mybir.AxisListType

ENGINES = ("sync", "tensor", "scalar", "vector", "gpsimd")
NT = 2048
D = 1024
NC8 = 8
TQ = 4
TW = 512


def make_identity(nc, identity):
    sq = identity.shape[0]
    nc.gpsimd.memset(identity, 0.0)
    nc.gpsimd.affine_select(
        out=identity, in_=identity,
        compare_op=ALU.not_equal,
        fill=1.0, base=0,
        pattern=[[-1, sq]], channel_multiplier=1,
    )


class B:

    def __init__(self, nc):
        self.nc = nc
        self.ops = {e: [] for e in ENGINES}
        self.sem_names = ["dl", "dg", "pe", "act", "vec", "pool"]
        self.cnt = {s: 0 for s in self.sem_names}
        self.waited = {e: {s: 0 for s in self.sem_names} for e in ENGINES}
        self.sems = {}

    def raw(self, eng, fn):
        self.ops[eng].append(fn)

    def op(self, eng, fn, inc=None, n=1):
        if inc is not None:
            self.cnt[inc] += n
        def run(e, fn=fn, inc=inc, n=n):
            ins = fn(e)
            if inc is not None:
                ins.then_inc(self.sems[inc], n)
        self.ops[eng].append(run)

    def wait(self, eng, sem, val=None):
        val = self.cnt[sem] if val is None else val
        if val <= 0 or val <= self.waited[eng][sem]:
            return
        self.waited[eng][sem] = val
        self.ops[eng].append(lambda e, s=sem, v=val: e.wait_ge(self.sems[s], v))

    def fence(self, engs=ENGINES):
        for e in engs:
            for s in self.sem_names:
                self.wait(e, s)

    def dma(self, queue, out, in_):
        sem = "dl" if queue == "sync" else "dg"
        self.op(queue, lambda e, o=out, i=in_: e.dma_start(out=o, in_=i), inc=sem, n=16)

    def mm(self, out, lhsT, rhs, start, stop, inc=False):
        self.op("tensor",
                lambda e, o=out, l=lhsT, r=rhs, st=start, sp=stop: e.matmul(o, l, r, start=st, stop=sp),
                inc="pe" if inc else None)

    def tr(self, out, in_, ident, inc=True):
        self.op("tensor",
                lambda e, o=out, i=in_, d=ident: e.matmul(o, i, d, start=True, stop=True,
                                                          is_transpose=True),
                inc="pe" if inc else None)

    def act(self, out, in_, func, bias=0.0, scale=1.0):
        self.wait("scalar", "act")
        self.op("scalar",
                lambda e, o=out, i=in_, f=func, b=bias, s=scale: e.activation(o, i, f, bias=b, scale=s),
                inc="act")

    def vop(self, fn):
        self.wait("vector", "vec")
        self.op("vector", fn, inc="vec")

    def pop(self, fn):
        self.op("gpsimd", fn, inc="pool")


def build_common(nc, stk):
    t = {}
    sb = lambda name, shape: stk.enter_context(nc.sbuf_tensor(name, shape, F32))
    ps = lambda name, shape: stk.enter_context(nc.psum_tensor(name, shape, F32))
    t["A"] = sb("A", [128, 16384])
    t["Bb"] = sb("Bb", [128, 16384])
    t["C"] = sb("C", [128, 8192])
    t["ident"] = sb("ident", [128, 128])
    t["onesrow"] = sb("onesrow", [1, 128])
    t["onescol"] = sb("onescol", [128, 1])
    t["wbuf0"] = sb("wbuf0", [128, 1024])
    t["wbuf1"] = t["wbuf0"]
    t["cols"] = sb("cols", [128, 128])
    t["bmean"] = sb("bmean", [128, 1024])
    t["brstd"] = sb("brstd", [128, 512])
    t["qh"] = stk.enter_context(nc.sbuf_tensor("qh", [32, 512], F32))
    t["kvcols"] = sb("kvcols", [128, 32])
    t["msb"] = sb("msb", [32, 64])
    t["PS_BIG"] = ps("PS_BIG", [128, 2048])
    t["PS_H"] = ps("PS_H", [128, 512])
    t["PS_F"] = ps("PS_F", [128, 512])
    t["PS_S1"] = ps("PS_S1", [1, 512])
    t["PS_S2"] = ps("PS_S2", [1, 512])
    return t


def setup_phase(b, t):
    b.pop(lambda e: e.memset(t["ident"].ap(), 0.0))
    b.op("gpsimd", lambda e: e.affine_select(
        out=t["ident"].ap(), in_=t["ident"].ap(), compare_op=ALU.not_equal,
        fill=1.0, base=0, pattern=[[-1, 128]], channel_multiplier=1), inc="pool")
    b.pop(lambda e: e.memset(t["onesrow"].ap(), 1.0))
    b.pop(lambda e: e.memset(t["onescol"].ap(), 1.0))
    b.fence()


def Achunk(t, c):
    return t["A"].ap()[:, 2048 * c:2048 * (c + 1)]


def Bchunk(t, c):
    return t["Bb"].ap()[:, 2048 * c:2048 * (c + 1)]


def Cslot(t, s):
    return t["C"].ap()[:, 2048 * s: 2048 * (s + 1)]


def colify(b, t, vec_ap, F, dst_off):
    cols = t["cols"].ap()
    rowbig = t["brstd"].ap()[0:1, 0:512]
    psh = t["PS_H"].ap()
    for j0 in range(0, F, 512):
        w = min(512, F - j0)
        nch = w // 128
        b.wait("sync", "pe")
        b.dma("sync", rowbig[0:1, 0:w], vec_ap[0:1, j0:j0 + w])
        b.wait("tensor", "dl")
        b.wait("tensor", "vec")
        for k in range(nch):
            b.mm(psh[:, k:k + 1], rowbig[0:1, 128 * k:128 * (k + 1)],
                 t["onesrow"].ap()[0:1, 0:1], start=True, stop=True,
                 inc=(k == nch - 1))
        b.wait("vector", "pe")
        b.vop(lambda e, d=cols[:, dst_off + j0 // 128: dst_off + j0 // 128 + nch],
              s=psh[:, 0:nch]: e.tensor_copy(d, s))


def ln_fm(b, t, bufap, eps, scale_cols=None, bias_cols=None, sqscr=None):
    ps1, ps2 = t["PS_S1"].ap(), t["PS_S2"].ap()
    psb = t["PS_BIG"].ap()
    mean_r = t["bmean"].ap()[0:1, 0:512]
    ex2_r = t["bmean"].ap()[0:1, 512:1024]
    scr_r = t["brstd"].ap()[0:1, 0:512]
    oc = t["onescol"].ap()
    orow = t["onesrow"].ap()
    for q in range(TQ):
        sl = slice(TW * q, TW * (q + 1))
        b.wait("tensor", "vec")
        for c in range(NC8):
            b.mm(ps1[0:1, :], oc, bufap(c)[:, sl], start=(c == 0), stop=(c == 7),
                 inc=(c == 7))
        for c in range(NC8):
            half = (c % 2) * TW
            scr = sqscr[:, half:half + TW]
            b.wait("scalar", "pe")
            b.act(scr, bufap(c)[:, sl], AF.Square)
            b.wait("tensor", "act")
            b.mm(ps2[0:1, :], oc, scr, start=(c == 0), stop=(c == 7), inc=True)
        b.wait("vector", "pe")
        b.vop(lambda e, o=mean_r, i=ps1[0:1, :]: e.tensor_scalar_mul(o, i, 1.0 / D))
        b.vop(lambda e, o=ex2_r, i=ps2[0:1, :]: e.tensor_scalar_mul(o, i, 1.0 / D))
        b.vop(lambda e, o=scr_r, a=mean_r: e.tensor_mul(o, a, a))
        b.vop(lambda e, o=ex2_r, a=ex2_r, c2=scr_r: e.tensor_sub(o, a, c2))
        b.vop(lambda e, o=ex2_r, i=ex2_r: e.tensor_scalar_add(o, i, float(eps)))
        b.wait("scalar", "vec")
        b.act(ex2_r, ex2_r, AF.Sqrt)
        b.wait("vector", "act")
        b.vop(lambda e, o=ex2_r, i=ex2_r: e.reciprocal(o, i))
        b.wait("tensor", "vec")
        b.mm(psb[:, 0:512], orow, mean_r, start=True, stop=True, inc=False)
        b.mm(psb[:, 512:1024], orow, ex2_r, start=True, stop=True, inc=True)
        b.wait("vector", "pe")
        for c in range(NC8):
            b.vop(lambda e, o=bufap(c)[:, sl], a=bufap(c)[:, sl], m=psb[:, 0:512]:
                  e.tensor_sub(o, a, m))
            b.vop(lambda e, o=bufap(c)[:, sl], a=bufap(c)[:, sl], r=psb[:, 512:1024]:
                  e.tensor_mul(o, a, r))
            if scale_cols is not None:
                b.vop(lambda e, o=bufap(c)[:, sl], a=bufap(c)[:, sl],
                      s=scale_cols(c), bb=bias_cols(c):
                      e.tensor_scalar(o, a, s, bb, ALU.mult, ALU.add))
    b.fence()


def open_ln(b, t, rin, dstchunk, eps, sqscr):
    for c in range(NC8):
        b.dma("sync", dstchunk(c), rin[128 * c:128 * (c + 1), :])
    b.wait("tensor", "dl")
    b.wait("scalar", "dl")
    b.wait("vector", "dl")
    ln_fm(b, t, dstchunk, eps, sqscr=sqscr)


def close_residual(b, t, ybufchunk, rin, rout, eps, sqscr):
    ln_fm(b, t, ybufchunk, eps, sqscr=sqscr)
    rs = t["bmean"].ap()[:, 512:1024]
    for c in range(NC8):
        for q in range(TQ):
            sl = slice(TW * q, TW * (q + 1))
            b.wait("sync", "vec")
            b.dma("sync", rs, rin[128 * c:128 * (c + 1), sl])
            b.wait("vector", "dl")
            b.vop(lambda e, o=ybufchunk(c)[:, sl], a=ybufchunk(c)[:, sl], r=rs:
                  e.tensor_add(o, a, r))
        b.wait("gpsimd", "vec")
        b.dma("gpsimd", rout[128 * c:128 * (c + 1), :], ybufchunk(c))
    b.fence()


def load_colblock(b, t, parity, w2d, o, nch=8):
    wb = t["wbuf%d" % parity]
    b.wait("sync", "pe")
    src = w2d[:, 128 * o:128 * (o + 1)].rearrange("(c p) o -> p c o", p=128)
    dst = wb.ap()[:, 0:128 * nch].rearrange("p (c o) -> p c o", c=nch)
    b.dma("sync", dst, src)
    return wb.ap()


def mm_dense(b, t, w2d, rhschunk, F, evac, nin=8):
    psb = t["PS_BIG"].ap()
    for o in range(F // 128):
        p = o % 2
        wb = load_colblock(b, t, p, w2d, o, nin)
        b.wait("tensor", "dl")
        b.wait("tensor", "act")
        b.wait("tensor", "vec")
        for c in range(nin):
            for q in range(TQ):
                b.mm(psb[:, TW * q:TW * (q + 1)], wb[:, 128 * c:128 * (c + 1)],
                     rhschunk(c)[:, TW * q:TW * (q + 1)],
                     start=(c == 0), stop=(c == nin - 1),
                     inc=(c == nin - 1 and q == TQ - 1))
        b.lastuse[("wb", p)] = b.cnt["pe"]
        evac(o, psb)


def emit_stage0(b, t, aps):
    nc = b.nc
    x, wqkv, wproj, bproj = aps["x"], aps["L0_attn_W_qkv"], aps["L0_attn_W_proj"], aps["L0_attn_b_proj"]
    r0, rout = aps["r0"], aps["rout"]
    ident = t["ident"].ap()
    psh, psf, psb = t["PS_H"].ap(), t["PS_F"].ap(), t["PS_BIG"].ap()
    C0, C1, C2, C3 = [Cslot(t, i) for i in range(4)]

    for tt in range(16):
        b.wait("sync", "pe")
        b.wait("sync", "vec")
        b.wait("sync", "dg")
        b.dma("sync", C0[:, 0:1024], x[128 * tt:128 * (tt + 1), :])
        b.wait("tensor", "dl")
        for c in range(NC8):
            slot = psh[:, 0:128] if c % 2 == 0 else psf[:, 0:128]
            b.wait("tensor", "vec")
            b.tr(slot, C0[:, 128 * c:128 * (c + 1)], ident)
            b.wait("vector", "pe")
            b.vop(lambda e, o=C1[:, 128 * c:128 * (c + 1)], s=slot: e.tensor_copy(o, s))
        b.wait("gpsimd", "vec")
        for c in range(NC8):
            b.dma("gpsimd", r0[128 * c:128 * (c + 1), 128 * tt:128 * (tt + 1)],
                  C1[:, 128 * c:128 * (c + 1)])
    b.fence()

    colify(b, t, bproj, 1024, 0)
    b.fence()

    open_ln(b, t, r0, lambda c: Achunk(t, c), 1e-5, C3[:, 0:1024])

    kscol = t["kvcols"].ap()[:, 0:1]
    qh = t["qh"].ap()
    msb = t["msb"].ap()[:, 0:33]
    for g in range(8):
        for part, dst, func in ((0, C0, AF.Relu), (1, C1, AF.Relu), (2, C2, AF.Copy)):
            b.wait("sync", "pe")
            b.wait("sync", "vec")
            b.wait("sync", "dg")
            b.dma("sync", C3[:, 0:1024],
                  wqkv[1024 * part + 128 * g:1024 * part + 128 * (g + 1), :])
            b.wait("tensor", "dl")
            for c in range(NC8):
                slot = psh[:, 0:128] if c % 2 == 0 else psf[:, 0:128]
                b.wait("tensor", "vec")
                b.tr(slot, C3[:, 128 * c:128 * (c + 1)], ident)
                b.wait("vector", "pe")
                b.vop(lambda e, o=t["wbuf0"].ap()[:, 128 * c:128 * (c + 1)], s=slot:
                      e.tensor_copy(o, s))
            b.wait("tensor", "act")
            b.wait("tensor", "vec")
            for c in range(NC8):
                for q in range(TQ):
                    b.mm(psb[:, TW * q:TW * (q + 1)],
                         t["wbuf0"].ap()[:, 128 * c:128 * (c + 1)],
                         Achunk(t, c)[:, TW * q:TW * (q + 1)],
                         start=(c == 0), stop=(c == 7),
                         inc=(c == 7 and q == TQ - 1))
            b.wait("scalar", "pe")
            b.act(dst, psb, func)
        b.wait("vector", "act")
        b.vop(lambda e, o=kscol, i=C1: e.reduce_sum(o, i, AX.X))
        for src in (C1, C2):
            for tt in range(16):
                slot = psh[:, 0:128] if tt % 2 == 0 else psf[:, 0:128]
                b.wait("tensor", "vec")
                b.tr(slot, src[:, 128 * tt:128 * (tt + 1)], ident)
                b.wait("vector", "pe")
                b.vop(lambda e, o=src[:, 128 * tt:128 * (tt + 1)], s=slot: e.tensor_copy(o, s))
        b.fence(["tensor", "vector", "gpsimd"])
        for j in range(4):
            h = 4 * g + j
            psm = psh[0:32, 0:32]
            b.wait("tensor", "vec")
            b.wait("tensor", "dg")
            for tt in range(16):
                b.mm(psm, C1[:, 128 * tt + 32 * j:128 * tt + 32 * (j + 1)],
                     C2[:, 128 * tt + 32 * j:128 * tt + 32 * (j + 1)],
                     start=(tt == 0), stop=(tt == 15), inc=(tt == 15))
            b.wait("vector", "pe")
            b.vop(lambda e, o=msb[0:32, 0:32], s=psm: e.tensor_copy(o, s))
            b.vop(lambda e, o=msb[0:32, 32:33], s=kscol[32 * j:32 * (j + 1), 0:1]:
                  e.tensor_copy(o, s))
            dcol = t["kvcols"].ap()[:, 1:2]
            for q in range(TQ):
                b.wait("gpsimd", "pe")
                b.dma("gpsimd", qh, C0[32 * j:32 * (j + 1), TW * q:TW * (q + 1)])
                b.wait("tensor", "dg")
                for t4 in range(4):
                    tt = 4 * q + t4
                    slot = psf[:, 0:64] if tt % 2 == 0 else psf[:, 256:320]
                    b.wait("tensor", "vec")
                    b.mm(slot[:, 0:33], qh[:, 128 * t4:128 * (t4 + 1)], msb[0:32, 0:33],
                         start=True, stop=True, inc=True)
                    b.wait("vector", "pe")
                    stg = t["brstd"].ap()[:, 0:33]
                    b.vop(lambda e, o=stg, s=slot[:, 0:33]: e.tensor_copy(o, s))
                    b.vop(lambda e, o=dcol, i=stg[:, 32:33]:
                          e.tensor_scalar_add(o, i, 1e-15))
                    b.vop(lambda e, o=dcol, i=dcol: e.reciprocal(o, i))
                    b.wait("vector", "vec")
                    b.vop(lambda e,
                          o=t["Bb"].ap()[:, 1024 * tt + 32 * h:1024 * tt + 32 * (h + 1)],
                          i=stg[:, 0:32], d=dcol:
                          e.tensor_scalar_mul(o, i, d))
        b.fence(["tensor", "vector", "scalar"])
    b.fence()

    for c in range(NC8):
        for tt in range(16):
            slot = psh[:, 0:128] if tt % 2 == 0 else psf[:, 0:128]
            b.wait("tensor", "vec")
            b.tr(slot, t["Bb"].ap()[:, 1024 * tt + 128 * c:1024 * tt + 128 * (c + 1)], ident)
            b.wait("vector", "pe")
            b.vop(lambda e, o=Achunk(t, c)[:, 128 * tt:128 * (tt + 1)], s=slot:
                  e.tensor_copy(o, s))
    b.fence()

    for o in range(NC8):
        b.wait("sync", "pe")
        b.wait("sync", "vec")
        b.wait("sync", "dg")
        b.dma("sync", C0[:, 0:1024], wproj[128 * o:128 * (o + 1), :])
        b.wait("tensor", "dl")
        for c in range(NC8):
            slot = psh[:, 0:128] if c % 2 == 0 else psf[:, 0:128]
            b.wait("tensor", "vec")
            b.tr(slot, C0[:, 128 * c:128 * (c + 1)], ident)
            b.wait("vector", "pe")
            b.vop(lambda e, o2=t["wbuf0"].ap()[:, 128 * c:128 * (c + 1)], s=slot:
                  e.tensor_copy(o2, s))
        b.wait("tensor", "vec")
        b.wait("tensor", "act")
        for c in range(NC8):
            for q in range(TQ):
                b.mm(psb[:, TW * q:TW * (q + 1)], t["wbuf0"].ap()[:, 128 * c:128 * (c + 1)],
                     Achunk(t, c)[:, TW * q:TW * (q + 1)],
                     start=(c == 0), stop=(c == 7), inc=(c == 7 and q == TQ - 1))
        b.wait("vector", "pe")
        b.vop(lambda e, o2=Bchunk(t, o), i=psb, bb=t["cols"].ap()[:, o:o + 1]:
              e.tensor_scalar_add(o2, i, bb))
    b.fence()

    close_residual(b, t, lambda c: Bchunk(t, c), r0, rout, 1e-5, C0[:, 0:1024])


def emit_stage1(b, t, aps):
    rin, rout = aps["rin"], aps["rout"]
    W, bias = aps["L0_ffn_W"], aps["L0_ffn_b"]
    ld, lu = aps["L0_ffn_lora_down"], aps["L0_ffn_lora_up"]
    da, ua = aps["L0_ffn_down_aux"], aps["L0_ffn_up_aux"]
    ident = t["ident"].ap()
    psh, psf, psb = t["PS_H"].ap(), t["PS_F"].ap(), t["PS_BIG"].ap()
    C0, C1, C2, C3 = [Cslot(t, i) for i in range(4)]
    daT = lambda c: C0[:, 1024 + 100 * c:1024 + 100 * (c + 1)]
    ldT = C0[0:128, 1874:1878]
    luT = C0[64:96, 1824:1874]
    uaT = C2[0:64, 1024:2048]

    colify(b, t, bias, 1024, 0)
    open_ln(b, t, rin, lambda c: Achunk(t, c), 1e-5, C1[:, 0:1024])

    b.wait("sync", "vec")
    b.wait("sync", "pe")
    b.dma("sync", C3[0:100, 0:1024], da[:, :])
    b.wait("tensor", "dl")
    for c in range(NC8):
        slot = psh[:, 0:128] if c % 2 == 0 else psf[:, 0:128]
        b.wait("tensor", "vec")
        b.tr(slot[:, 0:100], C3[0:100, 128 * c:128 * (c + 1)], ident[0:100, 0:100])
        b.wait("vector", "pe")
        b.vop(lambda e, o=daT(c), s=slot[:, 0:100]: e.tensor_copy(o, s))
    b.wait("sync", "pe")
    b.dma("sync", C3[0:4, 0:100], ld[:, :])
    b.wait("tensor", "dl")
    b.wait("tensor", "vec")
    b.tr(psh[0:100, 0:4], C3[0:4, 0:100], ident[0:4, 0:4])
    b.wait("vector", "pe")
    b.vop(lambda e, o=ldT[0:100, 0:4], s=psh[0:100, 0:4]: e.tensor_copy(o, s))
    b.wait("sync", "pe")
    b.wait("tensor", "vec")
    b.dma("sync", C3[0:50, 0:4], lu[:, :])
    b.wait("tensor", "dl")
    b.tr(psf[0:4, 0:50], C3[0:50, 0:4], ident[0:50, 0:50])
    b.wait("vector", "pe")
    b.vop(lambda e, o=luT[0:4, 0:50], s=psf[0:4, 0:50]: e.tensor_copy(o, s))
    for c in range(NC8):
        b.wait("sync", "pe")
        b.dma("sync", C3[0:128, 0:50], ua[128 * c:128 * (c + 1), :])
        b.wait("tensor", "dl")
        b.wait("tensor", "vec")
        slot = psh[:, 128:256] if c % 2 == 0 else psf[:, 128:256]
        b.tr(slot[0:50, 0:128], C3[0:128, 0:50], ident)
        b.wait("vector", "pe")
        b.vop(lambda e, o=uaT[0:50, 128 * c:128 * (c + 1)], s=slot[0:50, 0:128]:
              e.tensor_copy(o, s))
    b.fence()

    for c in range(NC8):
        for q in range(TQ):
            b.mm(psb[0:100, TW * q:TW * (q + 1)], daT(c)[:, 0:100],
                 Achunk(t, c)[:, TW * q:TW * (q + 1)],
                 start=(c == 0), stop=(c == 7), inc=(c == 7 and q == TQ - 1))
    b.wait("vector", "pe")
    b.vop(lambda e, o=C1[0:100, :], s=psb[0:100, :]: e.tensor_copy(o, s))
    b.wait("tensor", "vec")
    for q in range(TQ):
        b.mm(psh[0:4, 0:TW], ldT[0:100, 0:4], C1[0:100, TW * q:TW * (q + 1)],
             start=True, stop=True, inc=True)
        b.wait("vector", "pe")
        b.vop(lambda e, o=C2[64:68, TW * q:TW * (q + 1)], s=psh[0:4, 0:TW]:
              e.tensor_copy(o, s))
        b.wait("tensor", "vec")
    for q in range(TQ):
        b.mm(psf[0:50, 0:TW], luT[0:4, 0:50], C2[64:68, TW * q:TW * (q + 1)],
             start=True, stop=True, inc=True)
        b.wait("vector", "pe")
        b.vop(lambda e, o=C3[0:50, TW * q:TW * (q + 1)], s=psf[0:50, 0:TW]:
              e.tensor_copy(o, s))
        b.wait("tensor", "vec")
    b.fence()

    for o in range(NC8):
        b.wait("sync", "pe")
        b.wait("sync", "vec")
        b.dma("sync", C0[:, 0:1024], W[128 * o:128 * (o + 1), :])
        b.wait("tensor", "dl")
        for c in range(NC8):
            slot = psh[:, 0:128] if c % 2 == 0 else psf[:, 0:128]
            b.wait("tensor", "vec")
            b.tr(slot, C0[:, 128 * c:128 * (c + 1)], ident)
            b.wait("vector", "pe")
            b.vop(lambda e, o2=t["wbuf0"].ap()[:, 128 * c:128 * (c + 1)], s=slot:
                  e.tensor_copy(o2, s))
        b.wait("tensor", "vec")
        b.wait("tensor", "act")
        for c in range(NC8):
            for q in range(TQ):
                b.mm(psb[:, TW * q:TW * (q + 1)], t["wbuf0"].ap()[:, 128 * c:128 * (c + 1)],
                     Achunk(t, c)[:, TW * q:TW * (q + 1)],
                     start=(c == 0), stop=False, inc=False)
        for q in range(TQ):
            b.mm(psb[:, TW * q:TW * (q + 1)], uaT[0:50, 128 * o:128 * (o + 1)],
                 C3[0:50, TW * q:TW * (q + 1)],
                 start=False, stop=True, inc=(q == TQ - 1))
        b.wait("vector", "pe")
        b.vop(lambda e, o2=Bchunk(t, o), i=psb, bb=t["cols"].ap()[:, o:o + 1]:
              e.tensor_scalar_add(o2, i, bb))
    b.fence()
    if "dbgS" in aps:
        for cc, src in ((0, C1), (1, C2), (2, C3), (3, C0)):
            b.dma("gpsimd", aps["dbgS"][128 * cc:128 * (cc + 1), :], src)
        b.fence()
        dbg_dump(b, t, aps["dbgS2"], lambda c: Bchunk(t, c))
    close_residual(b, t, lambda c: Bchunk(t, c), rin, rout, 1e-5, C1[:, 0:1024])


def emit_stage2_block(b, t, aps, i, do_open, do_close):
    rin, rout = aps["rin"], aps["rout"]
    ident = t["ident"].ap()
    psh, psf, psb = t["PS_H"].ap(), t["PS_F"].ap(), t["PS_BIG"].ap()
    C0, C1, C2, C3 = [Cslot(t, i2) for i2 in range(4)]
    cols = t["cols"].ap()

    colify(b, t, aps["L1_attn_bq"][i:i + 1, :], 1024, 0)
    colify(b, t, aps["L1_attn_bk"][i:i + 1, :], 1024, 8)
    colify(b, t, aps["L1_attn_bv"][i:i + 1, :], 1024, 16)
    colify(b, t, aps["L1_attn_bo"][i:i + 1, :], 1024, 24)
    colify(b, t, aps["L1_attn_b2"][i:i + 1, :], 1024, 32)
    colify(b, t, aps["L1_attn_ln1_s"][i:i + 1, :], 1024, 40)
    colify(b, t, aps["L1_attn_ln1_b"][i:i + 1, :], 1024, 48)
    colify(b, t, aps["L1_attn_ln2_s"][i:i + 1, :], 1024, 56)
    colify(b, t, aps["L1_attn_ln2_b"][i:i + 1, :], 1024, 64)
    colify(b, t, aps["L1_attn_b1"][i:i + 1, :], 4096, 72)
    b.fence()

    if do_open:
        open_ln(b, t, rin, lambda c: Achunk(t, c), 1e-5, C0[:, 0:1024])
    else:
        for c in range(NC8):
            b.dma("sync", Achunk(t, c), rin[128 * c:128 * (c + 1), :])
        b.fence()

    S = lambda c: Achunk(t, c)
    O = lambda c: Bchunk(t, c)

    for o in range(NC8):
        for widx, wname, bcol0 in ((0, "L1_attn_Wq", 0), (1, "L1_attn_Wk", 8),
                                   (2, "L1_attn_Wv", 16)):
            p = widx % 2
            wb = load_colblock(b, t, p, aps[wname][i], o)
            b.wait("tensor", "dl")
            b.wait("tensor", "vec")
            b.wait("tensor", "act")
            for c in range(NC8):
                for q in range(TQ):
                    b.mm(psb[:, TW * q:TW * (q + 1)], wb[:, 128 * c:128 * (c + 1)],
                         S(c)[:, TW * q:TW * (q + 1)],
                         start=(c == 0), stop=(c == 7), inc=(c == 7 and q == TQ - 1))
            b.lastuse[("wb", p)] = b.cnt["pe"]
            bc = cols[:, bcol0 + o:bcol0 + o + 1]
            b.wait("vector", "pe")
            b.wait("scalar", "pe")
            if widx < 2:
                dst = C0 if widx == 0 else C1
                b.vop(lambda e, o2=C3, i=psb, bb=bc:
                      e.tensor_scalar(o2, i, bb, 0.0, ALU.add, ALU.min))
                b.vop(lambda e, o2=dst, i=psb, bb=bc:
                      e.tensor_scalar(o2, i, bb, 0.0, ALU.add, ALU.max))
                b.wait("scalar", "vec")
                b.act(C3, C3, AF.Exp)
                b.wait("vector", "act")
                b.vop(lambda e, o2=dst, a=dst, c2=C3: e.tensor_add(o2, a, c2))
            else:
                b.vop(lambda e, o2=C2, i=psb, bb=bc:
                      e.tensor_scalar_add(o2, i, bb))
        kv = t["kvcols"].ap()[:, 2:3]
        ks = t["kvcols"].ap()[:, 3:4]
        b.vop(lambda e, o2=C3, a=C1, c2=C2: e.tensor_mul(o2, a, c2))
        b.vop(lambda e, o2=kv, i=C3: e.reduce_sum(o2, i, AX.X))
        b.vop(lambda e, o2=ks, i=C1: e.reduce_sum(o2, i, AX.X))
        b.wait("vector", "vec")
        b.vop(lambda e, o2=C3, i=C0, s=ks: e.tensor_scalar(o2, i, s, 1e-6, ALU.mult, ALU.add))
        b.vop(lambda e, o2=C3, i=C3: e.reciprocal(o2, i))
        b.wait("vector", "vec")
        b.vop(lambda e, o2=O(o), i=C0, s=kv, c2=C3:
              e.scalar_tensor_tensor(o2, i, s, c2, ALU.mult, ALU.mult))
    b.fence()

    def evac_a(o, ps):
        bc = cols[:, 24 + o:24 + o + 1]
        b.wait("scalar", "pe")
        b.act(C0, ps, AF.Gelu_apprx_tanh, bias=bc)
        b.act(C1, C0, AF.Gelu_apprx_tanh)
        b.wait("vector", "act")
        b.vop(lambda e, o2=S(o), a=S(o), c2=C1: e.tensor_add(o2, a, c2))
    mm_dense(b, t, aps["L1_attn_Wo"][i], O, 1024, evac_a)
    b.fence()

    ln_fm(b, t, S, 1e-6,
          scale_cols=lambda c: cols[:, 40 + c:40 + c + 1],
          bias_cols=lambda c: cols[:, 48 + c:48 + c + 1],
          sqscr=C0[:, 0:1024])

    TC = 256
    w1, w2 = aps["L1_attn_W1"][i], aps["L1_attn_W2"][i]
    for q8 in range(8):
        sl = slice(TC * q8, TC * (q8 + 1))
        for ff in range(32):
            p = ff % 2
            wb = load_colblock(b, t, p, w1, ff)
            b.wait("tensor", "dl")
            b.wait("tensor", "act")
            for c in range(NC8):
                b.mm(psh[:, 0:TC], wb[:, 128 * c:128 * (c + 1)], S(c)[:, sl],
                     start=(c == 0), stop=(c == 7), inc=(c == 7))
            b.lastuse[("wb", p)] = b.cnt["pe"]
            b.wait("scalar", "pe")
            hs = t["C"].ap()[:, 2048 * (ff // 8) + TC * (ff % 8):
                             2048 * (ff // 8) + TC * (ff % 8) + TC]
            b.act(hs, psh[:, 0:TC], AF.Gelu_apprx_tanh, bias=cols[:, 72 + ff:72 + ff + 1])
        b.fence(["tensor", "scalar", "vector"])
        for half in range(2):
            for ff in range(32):
                b.wait("sync", "pe")
                wb = t["wbuf0"]
                b.dma("sync", wb.ap()[:, 0:1024], w2[128 * ff:128 * (ff + 1), :])
                b.wait("tensor", "dl")
                b.wait("tensor", "act")
                hs = t["C"].ap()[:, 2048 * (ff // 8) + TC * (ff % 8):
                                 2048 * (ff // 8) + TC * (ff % 8) + TC]
                for o4 in range(4):
                    o = 4 * half + o4
                    b.mm(psb[:, 512 * o4:512 * o4 + TC], wb.ap()[:, 128 * o:128 * (o + 1)],
                         hs, start=(ff == 0), stop=(ff == 31),
                         inc=(o4 == 3))
            b.wait("scalar", "pe")
            gt = t["brstd"].ap()[:, 0:TC]
            for o4 in range(4):
                o = 4 * half + o4
                b.act(gt, psb[:, 512 * o4:512 * o4 + TC], AF.Gelu_apprx_tanh,
                      bias=cols[:, 32 + o:32 + o + 1])
                b.wait("vector", "act")
                b.vop(lambda e, o2=O(o)[:, sl], a=S(o)[:, sl], c2=gt: e.tensor_add(o2, a, c2))
                b.wait("scalar", "vec")
        b.fence(["tensor", "scalar", "vector"])
        if q8 == 0 and "dbgF" in aps:
            for ffd in range(4):
                hsd = t["C"].ap()[:, 2048 * (ffd // 8) + 256 * (ffd % 8):
                                  2048 * (ffd // 8) + 256 * (ffd % 8) + 256]
                b.dma("gpsimd", aps["dbgF"][1024 + 128 * ffd:1024 + 128 * (ffd + 1), 0:256], hsd)
            b.fence()
    if "dbgF" in aps:
        dbg_dump(b, t, aps["dbgF"], O)
    ln_fm(b, t, O, 1e-6,
          scale_cols=lambda c: cols[:, 56 + c:56 + c + 1],
          bias_cols=lambda c: cols[:, 64 + c:64 + c + 1],
          sqscr=C0[:, 0:1024])

    if do_close:
        close_residual(b, t, O, aps["r_orig"], rout, 1e-5, C0[:, 0:1024])
    else:
        for c in range(NC8):
            b.wait("gpsimd", "vec")
            b.dma("gpsimd", rout[128 * c:128 * (c + 1), :], O(c))
        b.fence()


def emit_stage3(b, t, aps):
    rin, out = aps["rin"], aps["out"]
    ident = t["ident"].ap()
    psh, psf, psb = t["PS_H"].ap(), t["PS_F"].ap(), t["PS_BIG"].ap()
    C0, C1, C2, C3 = [Cslot(t, i) for i in range(4)]
    wg = aps["L1_ffn_Wg"]

    open_ln(b, t, rin, lambda c: Achunk(t, c), 1e-5, C0[:, 0:1024])
    U = lambda c: Achunk(t, c)

    b.dma("sync", t["cols"].ap()[:, 0:64].rearrange("p (c o) -> p c o", c=8),
          wg.rearrange("(c p) o -> p c o", p=128))
    b.fence()

    wtok = t["bmean"].ap()[:, 0:32]
    wTp = t["brstd"].ap()[0:32, 0:512]
    wrow = t["bmean"].ap()[0:1, 0:512]
    bw = t["bmean"].ap()[:, 512:1024]
    m1 = t["kvcols"].ap()[:, 4:5]
    m2 = t["kvcols"].ap()[:, 5:6]
    rec = t["kvcols"].ap()[:, 6:7]
    scr8 = t["kvcols"].ap()[:, 8:16]
    hs = lambda ff: t["C"].ap()[:, 2048 * (ff // 4) + 512 * (ff % 4):
                                2048 * (ff // 4) + 512 * (ff % 4) + 512]
    for q in range(TQ):
        sl = slice(TW * q, TW * (q + 1))
        b.wait("tensor", "vec")
        b.wait("tensor", "act")
        for c in range(NC8):
            b.mm(psh[0:8, 0:TW], t["cols"].ap()[:, 8 * c:8 * (c + 1)], U(c)[:, sl],
                 start=(c == 0), stop=(c == 7), inc=(c == 7))
        b.wait("scalar", "pe")
        b.act(psf[0:8, 0:TW], psh[0:8, 0:TW], AF.Sigmoid)
        b.wait("vector", "act")
        b.vop(lambda e, o=wTp[0:8, 0:TW], s=psf[0:8, 0:TW]: e.tensor_copy(o, s))
        if "dbgW" in aps:
            b.vop(lambda e, o=t["brstd"].ap()[32:40, 0:TW], s=psf[0:8, 0:TW]: e.tensor_copy(o, s))
        for t4 in range(4):
            slot = psh[:, 128:136] if t4 % 2 == 0 else psf[:, 128:136]
            b.wait("tensor", "vec")
            b.tr(slot, wTp[0:8, 128 * t4:128 * (t4 + 1)], ident[0:8, 0:8])
            b.wait("vector", "pe")
            gt = wtok[:, 8 * t4:8 * (t4 + 1)]
            b.vop(lambda e, o=gt, s=slot: e.tensor_copy(o, s))
            b.vop(lambda e, o=m1, i=gt: e.reduce_max(o, i, AX.X))
            b.wait("vector", "vec")
            b.vop(lambda e, o=scr8, i=gt, s=m1:
                  e.tensor_scalar(o, i, s, -1e30, ALU.is_ge, ALU.mult))
            b.vop(lambda e, o=scr8, a=scr8, c2=gt: e.tensor_add(o, a, c2))
            b.vop(lambda e, o=m2, i=scr8: e.reduce_max(o, i, AX.X))
            b.wait("vector", "vec")
            b.vop(lambda e, o=rec, a=m1, c2=m2: e.tensor_add(o, a, c2))
            b.vop(lambda e, o=rec, i=rec: e.reciprocal(o, i))
            b.wait("vector", "vec")
            b.vop(lambda e, o=scr8, i=gt, s=m2: e.tensor_scalar(o, i, s, 1.0, ALU.is_ge, ALU.mult))
            b.vop(lambda e, o=scr8, a=scr8, c2=gt: e.tensor_mul(o, a, c2))
            b.wait("vector", "vec")
            b.vop(lambda e, o=gt, i=scr8, s=rec: e.tensor_scalar_mul(o, i, s))
            slot2 = psh[0:8, 136:264] if t4 % 2 == 0 else psf[0:8, 136:264]
            b.wait("tensor", "vec")
            b.tr(slot2, gt, ident)
            b.wait("vector", "pe")
            b.vop(lambda e, o=wTp[0:8, 128 * t4:128 * (t4 + 1)], s=slot2[0:8, 0:128]:
                  e.tensor_copy(o, s))
        b.fence(["tensor", "vector", "scalar", "gpsimd", "sync"])
        if "dbgW" in aps:
            b.dma("gpsimd", aps["dbgW"][32 * q:32 * q + 32, 0:512], t["brstd"].ap()[0:40, 0:512][0:32, :])
            b.dma("gpsimd", aps["dbgW"][1152 + 32 * q:1152 + 32 * q + 8, 0:512], t["brstd"].ap()[32:40, 0:512])
            b.fence()
        for e_i in range(9):
            shared = (e_i == 0)
            if shared:
                w1a, w3a, w2a = aps["L1_ffn_Ws1"], aps["L1_ffn_Ws3"], aps["L1_ffn_Ws2"]
            else:
                w1a, w3a, w2a = (aps["L1_ffn_We1"][e_i - 1], aps["L1_ffn_We3"][e_i - 1],
                                 aps["L1_ffn_We2"][e_i - 1])
            if not shared:
                b.wait("gpsimd", "pe")
                b.wait("gpsimd", "vec")
                b.dma("gpsimd", wrow[0:1, 0:TW], wTp[e_i - 1:e_i, 0:TW])
                b.wait("tensor", "dg")
                b.wait("tensor", "vec")
                b.mm(psf[:, 0:TW], t["onesrow"].ap(), wrow[0:1, 0:TW],
                     start=True, stop=True, inc=True)
                b.wait("vector", "pe")
                b.vop(lambda e, o=bw, s=psf[:, 0:TW]: e.tensor_copy(o, s))
            for ff in range(16):
                p = ff % 2
                wb = load_colblock(b, t, p, w1a, ff)
                b.wait("tensor", "dl")
                b.wait("tensor", "act")
                b.wait("tensor", "vec")
                for c in range(NC8):
                    b.mm(psh[:, 0:TW], wb[:, 128 * c:128 * (c + 1)], U(c)[:, sl],
                         start=(c == 0), stop=(c == 7), inc=(c == 7))
                b.lastuse[("wb", p)] = b.cnt["pe"]
                wb3 = load_colblock(b, t, p, w3a, ff)
                b.wait("tensor", "dl")
                for c in range(NC8):
                    b.mm(psf[:, 0:TW], wb3[:, 128 * c:128 * (c + 1)], U(c)[:, sl],
                         start=(c == 0), stop=(c == 7), inc=(c == 7))
                b.lastuse[("wb", p)] = b.cnt["pe"]
                b.wait("scalar", "pe")
                b.act(hs(ff), psh[:, 0:TW], AF.Silu)
                b.wait("vector", "act")
                b.vop(lambda e, o=hs(ff), a=hs(ff), c2=psf[:, 0:TW]: e.tensor_mul(o, a, c2))
                if not shared:
                    b.vop(lambda e, o=hs(ff), a=hs(ff), c2=bw: e.tensor_mul(o, a, c2))
                b.wait("tensor", "vec")
            for half in range(2):
                for ff in range(16):
                    p = ff % 2
                    key = ("wb2", p)
                    b.wait("sync", "pe")
                    wb = t["wbuf%d" % p]
                    b.dma("sync", wb.ap()[:, 0:1024], w2a[128 * ff:128 * (ff + 1), :])
                    b.wait("tensor", "dl")
                    for o4 in range(4):
                        o = 4 * half + o4
                        b.mm(psb[:, TW * o4:TW * (o4 + 1)],
                             wb.ap()[:, 128 * o:128 * (o + 1)], hs(ff),
                             start=(ff == 0), stop=(ff == 15),
                             inc=(o4 == 3))
                    b.lastuse[key] = b.cnt["pe"]
                b.wait("vector", "pe")
                for o4 in range(4):
                    o = 4 * half + o4
                    if e_i == 0:
                        b.vop(lambda e, o2=Bchunk(t, o)[:, sl], s=psb[:, TW * o4:TW * (o4 + 1)]:
                              e.tensor_copy(o2, s))
                    else:
                        b.vop(lambda e, o2=Bchunk(t, o)[:, sl], a=Bchunk(t, o)[:, sl],
                              s=psb[:, TW * o4:TW * (o4 + 1)]:
                              e.tensor_add(o2, a, s))
                b.wait("tensor", "vec")
    b.fence()
    if "dbgW" in aps:
        for cc in range(NC8):
            b.dma("gpsimd", aps["dbgW"][128 + 128 * cc:256 + 128 * cc, :],
                  Bchunk(t, cc))
        b.fence()
    for c in range(NC8):
        b.vop(lambda e, o=Bchunk(t, c), i=Bchunk(t, c): e.tensor_scalar_mul(o, i, 1.0 / 3.0))
    b.fence()
    ln_fm(b, t, lambda c: Bchunk(t, c), 1e-5, sqscr=C0[:, 0:1024])
    rs = t["bmean"].ap()[:, 512:1024]
    for c in range(NC8):
        for q in range(TQ):
            sl = slice(TW * q, TW * (q + 1))
            b.wait("sync", "vec")
            b.dma("sync", rs, rin[128 * c:128 * (c + 1), sl])
            b.wait("vector", "dl")
            b.vop(lambda e, o=Bchunk(t, c)[:, sl], a=Bchunk(t, c)[:, sl], r=rs:
                  e.tensor_add(o, a, r))
    b.fence()
    for c in range(NC8):
        for tt in range(16):
            slot = psh[:, 0:128] if tt % 2 == 0 else psf[:, 0:128]
            b.wait("tensor", "vec")
            b.wait("tensor", "dg")
            b.tr(slot, Bchunk(t, c)[:, 128 * tt:128 * (tt + 1)], ident)
            b.wait("vector", "pe")
            b.vop(lambda e, o=C0[:, 128 * c:128 * (c + 1)], s=slot: e.tensor_copy(o, s))
            b.wait("gpsimd", "vec")
            b.dma("gpsimd", out[128 * tt:128 * (tt + 1), 128 * c:128 * (c + 1)],
                  C0[:, 128 * c:128 * (c + 1)])
    b.fence()


def finish(nc, b, stk):
    import contextlib
    with contextlib.ExitStack() as sems:
        for s in b.sem_names:
            b.sems[s] = sems.enter_context(nc.semaphore(s))
        with nc.Block() as block:
            @block.sync
            def _(eng):
                for f in b.ops["sync"]:
                    f(eng)

            @block.tensor
            def _(eng):
                for f in b.ops["tensor"]:
                    f(eng)

            @block.scalar
            def _(eng):
                for f in b.ops["scalar"]:
                    f(eng)

            @block.vector
            def _(eng):
                for f in b.ops["vector"]:
                    f(eng)

            @block.gpsimd
            def _(eng):
                for f in b.ops["gpsimd"]:
                    f(eng)


def dbg_dump(b, t, dst, chunkfn):
    b.fence()
    for c in range(NC8):
        b.dma("gpsimd", dst[128 * c:128 * (c + 1), :], chunkfn(c))
    b.fence()


def decl(nc, name, arr_shape, kind):
    return nc.dram_tensor(name, list(arr_shape), F32, kind=kind).ap()


def build_program(stage, weights_shapes):
    import contextlib
    nc = bass.Bass("TRN2", target_bir_lowering=False, debug=False)
    aps = {}
    names = []

    def ein(name, shape):
        aps[name] = decl(nc, name, shape, "ExternalInput")
        names.append(name)

    if stage == "s0":
        ein("x", (2048, 1024))
        ein("L0_attn_W_qkv", (3072, 1024))
        ein("L0_attn_W_proj", (1024, 1024))
        ein("L0_attn_b_proj", (1, 1024))
        aps["r0"] = nc.dram_tensor("r0", [1024, 2048], F32).ap()
        aps["rout"] = decl(nc, "rout", (1024, 2048), "ExternalOutput")
    elif stage == "s1":
        ein("rin", (1024, 2048))

        ein("L0_ffn_W", (1024, 1024))
        ein("L0_ffn_b", (1, 1024))
        ein("L0_ffn_lora_down", (4, 100))
        ein("L0_ffn_lora_up", (50, 4))
        ein("L0_ffn_down_aux", (100, 1024))
        ein("L0_ffn_up_aux", (1024, 50))
        aps["rout"] = decl(nc, "rout", (1024, 2048), "ExternalOutput")
        import os as _os
        if _os.environ.get("KDBG") == "s1":
            aps["dbgS"] = decl(nc, "dbgS", (512, 2048), "ExternalOutput")
            aps["dbgS2"] = decl(nc, "dbgS2", (1024, 2048), "ExternalOutput")
    elif stage.startswith("s2"):
        i = int(stage[-1])
        ein("rin", (1024, 2048))
        if i == 3:
            ein("r_orig", (1024, 2048))
        for nm in ["Wq", "Wk", "Wv", "Wo"]:
            ein("L1_attn_" + nm, (4, 1024, 1024))
        ein("L1_attn_W1", (4, 1024, 4096))
        ein("L1_attn_W2", (4, 4096, 1024))
        for nm in ["bq", "bk", "bv", "bo", "ln1_s", "ln1_b", "b2", "ln2_s", "ln2_b"]:
            ein("L1_attn_" + nm, (4, 1024))
        ein("L1_attn_b1", (4, 4096))
        aps["rout"] = decl(nc, "rout", (1024, 2048), "ExternalOutput")
        import os as _os
        if _os.environ.get("KDBG") == "s2":
            aps["dbgF"] = decl(nc, "dbgF", (1536, 2048), "ExternalOutput")
    elif stage == "s3":
        ein("rin", (1024, 2048))
        ein("L1_ffn_Wg", (1024, 8))
        ein("L1_ffn_Ws1", (1024, 2048))
        ein("L1_ffn_Ws3", (1024, 2048))
        ein("L1_ffn_Ws2", (2048, 1024))
        ein("L1_ffn_We1", (8, 1024, 2048))
        ein("L1_ffn_We3", (8, 1024, 2048))
        ein("L1_ffn_We2", (8, 2048, 1024))
        aps["out"] = decl(nc, "out", (2048, 1024), "ExternalOutput")
        import os as _os
        if _os.environ.get("KDBG") == "s3":
            aps["dbgW"] = decl(nc, "dbgW", (1280, 2048), "ExternalOutput")

    with contextlib.ExitStack() as stk:
        t = build_common(nc, stk)
        b = B(nc)
        b.lastuse = {}
        setup_phase(b, t)
        if stage == "s0":
            emit_stage0(b, t, aps)
        elif stage == "s1":
            emit_stage1(b, t, aps)
        elif stage.startswith("s2"):
            i = int(stage[-1])
            emit_stage2_block(b, t, aps, i, do_open=(i == 0), do_close=(i == 3))
        elif stage == "s3":
            emit_stage3(b, t, aps)
        finish(nc, b, stk)
    return nc, names


_prog_cache = {}


def get_prog(stage):
    if stage not in _prog_cache:
        _prog_cache[stage] = build_program(stage, None)
    return _prog_cache[stage]


def kernel(**inputs):
    xs = np.ascontiguousarray(inputs["L0_attn_x"].astype(np.float32))
    f = lambda k: np.ascontiguousarray(np.asarray(inputs[k], dtype=np.float32))
    CORES = list(range(8))

    def run(stage, per_core_extra, shared_keys, out_key):
        nc, names = get_prog(stage)
        shared = {}
        for k in names:
            if k in shared_keys:
                a = f(k)
                if a.ndim == 1:
                    a = a.reshape(1, -1)
                shared[k] = a
        in_maps = []
        for ci in CORES:
            m = dict(shared)
            m.update(per_core_extra[ci])
            in_maps.append(m)
        res = run_bass_kernel_spmd(nc, in_maps, core_ids=CORES)
        return [res.results[ci][out_key] for ci in range(8)]

    wk = set(inputs.keys())
    r = run("s0", [{"x": xs[ci]} for ci in CORES], wk, "rout")
    r = run("s1", [{"rin": r[ci]} for ci in CORES], wk, "rout")
    r_orig = r
    for i in range(4):
        extra = [{"rin": r[ci]} for ci in CORES]
        if i == 3:
            for ci in CORES:
                extra[ci]["r_orig"] = r_orig[ci]
        r = run("s2_%d" % i, extra, wk, "rout")
    o = run("s3", [{"rin": r[ci]} for ci in CORES], wk, "out")
    return np.stack(o, axis=0)
